# version 02 (final)
# baseline (speedup 1.0000x reference)
import numpy as np
import concourse.bass as bass
import concourse.mybir as mybir
from concourse.bass_utils import run_bass_kernel_spmd
from contextlib import ExitStack

F32 = mybir.dt.float32
U32 = mybir.dt.uint32
AF = mybir.ActivationFunctionType
ALU = mybir.AluOpType
AX = mybir.AxisListType

B, S, DM = 4, 2048, 2048
H, HKV, D = 16, 4, 128
NT = 8
E, IMOE, ISH = 16, 1408, 5632
NIC = IMOE // 128
NPE = E + ISH // IMOE
KC = DM // 128
SCALE = float(D) ** -0.5
CLAMP = 0.95
EPS = 1e-5


class Sem:
    def __init__(self, name):
        self.name, self.n, self.h = name, 0, None


class Eng:
    def __init__(self, name):
        self.name, self.steps = name, []


def build_nc(debug=()):
    debug = set(debug)
    nc = bass.Bass("TRN2", target_bir_lowering=False, detect_race_conditions=False)

    di = lambda n, sh, dt=F32: nc.dram_tensor(n, sh, dt, kind="ExternalInput")
    do = lambda n, sh, dt=F32: nc.dram_tensor(n, sh, dt, kind="ExternalOutput")
    xs = di("xs", [1024, DM])
    xp = di("xp", [1024, DM])
    pscale = di("pscale", [128, 1])
    wq_r = di("wq_r", [H, 128, KC * 128])
    wkv_r = di("wkv_r", [2 * HKV, 128, KC * 128])
    bq2 = di("bq2", [128, 16])
    bk2 = di("bk2", [128, 4])
    bv2 = di("bv2", [128, 4])
    rwT = di("rwT", [DM, E])
    wg_r = di("wg_r", [E * NIC + 44, 128, KC * 128])
    wu_r = di("wu_r", [E * NIC + 44, 128, KC * 128])
    wd = di("wd", [E, IMOE, DM])
    shd = di("shd", [ISH, DM])
    sgw2 = di("sgw2", [128, 16])
    ident_i = di("ident", [128, 128])
    masku_i = di("masku", [64, 64])
    sel_i = di("sel_all", [17, 17 * 128])
    out = do("out", [1024, DM])
    x2sp = nc.dram_tensor("x2sp", [1024, DM], F32, kind="Internal")
    dbg_t = {}
    for name, sh in (("d_h1f", [128, 2048]), ("d_kf", [128, 512]), ("d_P", [128, 512]),
                     ("d_st", [128, 512]), ("d_attn", [128, 16384]), ("d_x2", [128, 16384]),
                     ("d_hfm", [128, 16384]), ("d_prb", [128, 128]), ("d_cmbf", [17, 1024]),
                     ("d_cbc", [128, 1024]), ("d_ztc", [128, 11264]), ("d_ffn", [128, 16384]),
                     ("d_y0", [128, 16384])):
        if name in debug:
            dbg_t[name] = do(name, sh)

    ctx = ExitStack()
    sb = lambda n, sh, dt=F32: ctx.enter_context(nc.sbuf_tensor(n, sh, dt))
    with ctx:
        A = sb("bufA", [128, 16384])
        C = sb("bufC", [128, 16384])
        P = sb("bufP", [128, 16384])
        ident = sb("identS", [128, 128])
        masku = sb("maskuS", [64, 64])
        selE = sb("selE", [17, 128])
        bq2s = sb("bq2s", [128, 16]); sbq = sb("sbq", [128, 16])
        bk2s = sb("bk2s", [128, 4]); bv2s = sb("bv2s", [128, 4])
        sgw2s = sb("sgw2s", [128, 16])
        rwTs = sb("rwTs", [128, KC * E])
        psc = sb("psc", [128, 1])
        eps1 = sb("eps1", [128, 1]); one1 = sb("one1", [128, 1])
        z64 = sb("z64", [128, 64])
        GAP = sb("GAPt", [128, 8])
        ST8 = sb("ST8", [128, 512])
        PRB = sb("PRB", [128, 128])
        WKA = sb("WKA", [128, 128]); WKB = sb("WKB", [128, 128])
        DDX = sb("DDX", [128, 128])
        M4 = sb("M4", [128, 32])
        SM4 = sb("SM4", [128, 24])
        D4 = sb("D4", [128, 128])
        SGN, K1 = WKA, WKB
        CMBX = sb("CMBX", [128, 8 * 17])
        CMBF = sb("CMBF", [17, 1024])
        SQD = sb("SQD", [128, 512])
        pb = [ctx.enter_context(nc.psum_tensor(f"pb{i}", [128, 512], F32)) for i in range(8)]

        smT, smV, smS, smG, smD = (Sem(x) for x in ("smT", "smV", "smS", "smG", "smD"))
        for s_ in (smT, smV, smS, smG, smD):
            s_.h = ctx.enter_context(nc.semaphore(s_.name))
        T, V, Sc, G, Dq = Eng("T"), Eng("V"), Eng("S"), Eng("G"), Eng("D")

        def emit(ep, fn, waits=(), inc=None, amt=1):
            w = [(s, int(v)) for (s, v) in waits if v > 0]
            ep.steps.append((w, fn, inc, amt))
            if inc is not None:
                inc.n += amt
            return inc.n if inc is not None else None

        def dma(dst, src, waits=()):
            return emit(Dq, lambda e, d=dst, s=src: e.dma_start(out=d, in_=s), waits, smD, 16)

        VG = {"g": 0}

        def vgap():
            emit(V, lambda e: e.tensor_mul(GAP[:, 0:1], GAP[:, 2:3], GAP[:, 3:4]),
                 [(smG, VG["g"])])

        bank = [{"w": (smT, 0), "r": (smV, 0)} for _ in range(8)]
        slotR = {k: (smT, 0) for k in range(6)}

        def mm(bi, out_ap, lhsT, rhs, start, stop, extra_waits=(), force_inc=False):
            w = list(extra_waits)
            if start:
                w.append(bank[bi]["r"])
            emit(T, lambda e, o=out_ap, l=lhsT, r=rhs, st=start, sp=stop:
                 e.matmul(o, l, r, start=st, stop=sp), w,
                 smT if (stop or force_inc) else None, 1)
            if stop:
                bank[bi]["w"] = (smT, smT.n)

        def tpose(bi, out_ap, in_ap, extra_waits=()):
            emit(T, lambda e, o=out_ap, i=in_ap: e.transpose(o, i, ident[:]),
                 list(extra_waits) + [bank[bi]["r"], (smD, d_ident)], smT, 1)
            bank[bi]["w"] = (smT, smT.n)

        def evac(ep, sem, bi, fn, extra_waits=()):
            emit(ep, fn, list(extra_waits) + [bank[bi]["w"]], sem, 1)
            bank[bi]["r"] = (sem, sem.n)
            return sem.n

        def wslice(t, rowpitch, off0, col0, ncols):
            return bass.AP(t, off0 + col0,
                           [[rowpitch, 128], [128 * rowpitch, KC], [1, ncols]])

        emit(G, lambda e: e.memset(eps1[:], EPS), [], smG, 1)
        emit(G, lambda e: e.memset(one1[:], 1.0), [], smG, 1)
        emit(G, lambda e: e.memset(z64[:], 0.0), [], smG, 1)
        emit(G, lambda e: e.memset(GAP[:], 1.0), [], smG, 1)
        emit(G, lambda e: e.memset(P[:, 8896:9920], 0.0), [], smG, 1)
        g_init = smG.n
        VG["g"] = g_init
        dma(ident[:], ident_i[0:128, :])
        d_ident = smD.n
        dma(masku[:], masku_i[0:64, :])
        dma(bq2s[:], bq2[:, :]); dma(bk2s[:], bk2[:, :]); dma(bv2s[:], bv2[:, :])
        dma(sgw2s[:], sgw2[:, :]); dma(psc[:], pscale[:, :])
        dma(rwTs[:], wslice(rwT, E, 0, 0, E))
        d_consts = smD.n
        for j in range(NT):
            dma(A[:, j * 2048:(j + 1) * 2048], xs[j * 128:(j + 1) * 128, :])
        d_xtiles = smD.n
        emit(V, lambda e: e.tensor_scalar(sbq[:], bq2s[:], SCALE, None, ALU.mult),
             [(smD, d_consts)], smV, 1)
        vgap()

        H1F, OF = 0, 2048
        KF_, KFC, AA, PP, PIN, KT_, VF = 4096, 4608, 5120, 5632, 6144, 6656, 7168
        KTT, VTT = 7680, 8192
        QH1, SCC, STO, QT16 = 8704, 8832, 8896, 9920
        WSL = [11968, 14016]
        ZTC0, WSF, CBC0 = 0, [11264, 13312], 15360
        c_SPQ, c_SUM, c_T1, c_T2, c_SQ, c_MEAN, c_MSQ, c_VAR, c_VE, c_SD, c_R0, \
            c_RS, c_MR, c_NMR = 0, 32, 40, 48, 56, 64, 72, 80, 88, 96, 104, 112, 120, 128
        NCOL = 144
        R_NMX, R_NMN = 432, 440

        st_par = [0] * HKV

        def ln_batch_stats(srcs, slot):
            o = slot * NCOL
            for j, (src, w) in enumerate(srcs):
                for q in range(4):
                    emit(Sc, lambda e, q=q, src=src, o=o, j=j: e.activation(
                        SQD[:], src[:, q * 512:(q + 1) * 512], AF.Square,
                        accum_out=ST8[:, o + c_SPQ + q * 8 + j:o + c_SPQ + q * 8 + j + 1]),
                        list(w) if q == 0 else [], smS, 1)
                emit(V, lambda e, src=src, o=o, j=j: e.reduce_sum(
                    ST8[:, o + c_SUM + j:o + c_SUM + j + 1], src, axis=AX.X),
                    list(w), smV, 1)
            vgap()
            return smS.n, smV.n

        def ln_batch_finish(slot, s_sq, v_sum):
            o = slot * NCOL
            bl = lambda c: ST8[:, o + c:o + c + 8]
            emit(G, lambda e: e.tensor_add(bl(c_T1), ST8[:, o:o + 8], ST8[:, o + 8:o + 16]),
                 [(smS, s_sq), (smV, v_sum)], smG, 1)
            emit(G, lambda e: e.tensor_add(bl(c_T2), ST8[:, o + 16:o + 24],
                                           ST8[:, o + 24:o + 32]), [], smG, 1)
            emit(G, lambda e: e.tensor_add(bl(c_SQ), bl(c_T1), bl(c_T2)), [], smG, 1)
            emit(G, lambda e: e.tensor_scalar(bl(c_MSQ), bl(c_SQ), 1.0 / DM, None,
                                              ALU.mult), [], smG, 1)
            emit(G, lambda e: e.tensor_scalar(bl(c_MEAN), bl(c_SUM), 1.0 / DM, None,
                                              ALU.mult), [], smG, 1)
            emit(G, lambda e: e.tensor_mul(bl(c_T1), bl(c_MEAN), bl(c_MEAN)), [], smG, 1)
            emit(G, lambda e: e.tensor_sub(bl(c_VAR), bl(c_MSQ), bl(c_T1)), [], smG, 1)
            emit(G, lambda e: e.tensor_scalar(bl(c_VE), bl(c_VAR), EPS, None, ALU.add),
                 [], smG, 1)
            g_ve = smG.n
            emit(Sc, lambda e: e.activation(bl(c_SD), bl(c_VE), AF.Sqrt, bias=0.0,
                                            scale=1.0), [(smG, g_ve)], smS, 1)
            s_sd = smS.n
            emit(V, lambda e: e.reciprocal(bl(c_R0), bl(c_SD)), [(smS, s_sd)], smV, 1)
            vgap()
            v_r0 = smV.n
            emit(G, lambda e: e.tensor_mul(bl(c_T1), bl(c_R0), bl(c_R0)),
                 [(smV, v_r0)], smG, 1)
            emit(G, lambda e: e.tensor_mul(bl(c_T2), bl(c_T1), bl(c_VE)), [], smG, 1)
            emit(G, lambda e: e.tensor_scalar(bl(c_T1), bl(c_T2), -0.5, None, ALU.mult),
                 [], smG, 1)
            emit(G, lambda e: e.tensor_scalar(bl(c_T2), bl(c_T1), 1.5, None, ALU.add),
                 [], smG, 1)
            emit(G, lambda e: e.tensor_mul(bl(c_RS), bl(c_R0), bl(c_T2)), [], smG, 1)
            emit(G, lambda e: e.tensor_mul(bl(c_MR), bl(c_MEAN), bl(c_RS)), [], smG, 1)
            emit(G, lambda e: e.tensor_scalar(bl(c_NMR), bl(c_MR), -1.0, None, ALU.mult),
                 [], smG, 1)
            return smG.n

        def ln_apply(dst, src, slot, j, g_rdy, psum_banks=(), extra_waits=()):
            o = slot * NCOL
            w = [(smG, g_rdy)] + list(extra_waits)
            for bi in psum_banks:
                w.append(bank[bi]["r"])
            emit(Sc, lambda e, d=dst, s=src, o=o, j=j: e.activation(
                d, s, AF.Identity, bias=ST8[:, o + c_NMR + j:o + c_NMR + j + 1],
                scale=ST8[:, o + c_RS + j:o + c_RS + j + 1]), w, smS, 1)
            for bi in psum_banks:
                bank[bi]["w"] = (smS, smS.n)
            return smS.n

        def attn_tile(is_prefix, glob_chunk, x_src, x_wait, j, g_rdy):
            if is_prefix:
                dump = P[:, QT16:QT16 + 2048]
            else:
                dump = C[:, glob_chunk * 2048:(glob_chunk + 1) * 2048]
            s_ap = ln_apply(dump, x_src, 0, j, g_rdy,
                            extra_waits=[x_wait, (smT, smT.n)])
            for f in range(KC):
                bi = 2 + (f % 2)
                tpose(bi, pb[bi][0:128, 0:128], dump[:, f * 128:(f + 1) * 128],
                      extra_waits=[(smS, s_ap)])
                evac(Sc, smS, bi, lambda e, f=f, bi=bi: e.activation(
                    P[:, H1F + f * 128:H1F + (f + 1) * 128], pb[bi][0:128, 0:128],
                    AF.Identity, bias=0.0, scale=1.0))
            s_h1 = smS.n
            if "d_h1f" in dbg_t and not is_prefix and glob_chunk == 0:
                dma(dbg_t["d_h1f"][:, :], P[:, H1F:H1F + 2048], waits=[(smS, s_h1)])

            for g in range(HKV):
                slot = P[:, WSL[0]:WSL[0] + 2048]
                dk = dma(slot, bass.AP(wkv_r, g * 128 * 2048, [[2048, 128], [1, 2048]]),
                         waits=[slotR[0]])
                bi = g % 2
                for kc in range(KC):
                    mm(bi, pb[bi][0:128, 0:128], slot[:, kc * 128:(kc + 1) * 128],
                       P[:, H1F + kc * 128:H1F + (kc + 1) * 128], kc == 0, kc == KC - 1,
                       extra_waits=([(smD, dk), (smS, s_h1)] if kc == 0 else []),
                       force_inc=(kc == KC - 1))
                slotR[0] = (smT, smT.n)
                evac(Sc, smS, bi, lambda e, g=g, bi=bi: e.activation(
                    P[:, KF_ + g * 128:KF_ + (g + 1) * 128], pb[bi][0:128, 0:128],
                    AF.Sigmoid, bias=bk2s[:, g:g + 1], scale=1.0))
                slot2 = P[:, WSL[1]:WSL[1] + 2048]
                dv = dma(slot2, bass.AP(wkv_r, (HKV + g) * 128 * 2048,
                                        [[2048, 128], [1, 2048]]),
                         waits=[slotR[1]])
                for kc in range(KC):
                    mm(bi, pb[bi][0:128, 128:256], slot2[:, kc * 128:(kc + 1) * 128],
                       P[:, H1F + kc * 128:H1F + (kc + 1) * 128], kc == 0, kc == KC - 1,
                       extra_waits=([(smD, dv)] if kc == 0 else []),
                       force_inc=(kc == KC - 1))
                slotR[1] = (smT, smT.n)
                evac(Sc, smS, bi, lambda e, g=g, bi=bi: e.activation(
                    P[:, VF + g * 128:VF + (g + 1) * 128], pb[bi][0:128, 128:256],
                    AF.Identity, bias=bv2s[:, g:g + 1], scale=1.0))
            s_kv = smS.n
            for g in range(HKV):
                emit(V, lambda e, g=g: e.tensor_scalar_min(
                    P[:, KFC + g * 128:KFC + (g + 1) * 128],
                    P[:, KF_ + g * 128:KF_ + (g + 1) * 128], CLAMP),
                    [(smS, s_kv)] if g == 0 else [], smV, 1)
            v_kfc = smV.n
            for g in range(HKV):
                emit(Sc, lambda e, g=g: e.activation(
                    P[:, AA + g * 128:AA + (g + 1) * 128],
                    P[:, KFC + g * 128:KFC + (g + 1) * 128],
                    AF.Identity, bias=one1[:], scale=-1.0),
                    [(smV, v_kfc), (smG, g_init)] if g == 0 else [], smS, 1)
            s_aa = smS.n
            for g in range(HKV):
                for c in range(2):
                    emit(V, lambda e, g=g, c=c: e.tensor_tensor_scan(
                        P[:, PP + g * 128 + c * 64:PP + g * 128 + (c + 1) * 64],
                        P[:, AA + g * 128 + c * 64:AA + g * 128 + (c + 1) * 64],
                        z64[:], 1.0, ALU.mult, ALU.add),
                        [(smS, s_aa), (smG, g_init)] if (g == 0 and c == 0) else [], smV, 1)
            for g in range(HKV):
                emit(V, lambda e, g=g: e.reciprocal(
                    P[:, PIN + g * 128:PIN + (g + 1) * 128],
                    P[:, PP + g * 128:PP + (g + 1) * 128]), [], smV, 1)
            for g in range(HKV):
                emit(V, lambda e, g=g: e.tensor_mul(
                    P[:, KT_ + g * 128:KT_ + (g + 1) * 128],
                    P[:, KFC + g * 128:KFC + (g + 1) * 128],
                    P[:, PIN + g * 128:PIN + (g + 1) * 128]), [], smV, 1)
            v_kt = smV.n
            if "d_kf" in dbg_t and not is_prefix and glob_chunk == 0:
                dma(dbg_t["d_kf"][:, :], P[:, KFC:KFC + 512], waits=[(smV, v_kt)])
                dma(dbg_t["d_P"][:, :], P[:, PP:PP + 512], waits=[(smV, v_kt)])
            for g in range(HKV):
                for c in range(2):
                    emit(V, lambda e, g=g, c=c: e.tensor_scalar_mul(
                        P[:, PIN + g * 128 + c * 64:PIN + g * 128 + (c + 1) * 64],
                        P[:, KT_ + g * 128 + c * 64:KT_ + g * 128 + (c + 1) * 64],
                        P[:, PP + g * 128 + c * 64 + 63:PP + g * 128 + c * 64 + 64]),
                        [], smV, 1)
            v_kh = smV.n
            for g in range(HKV):
                bi = 2 + (g % 2)
                tpose(bi, pb[bi][0:128, 128:256], P[:, PIN + g * 128:PIN + (g + 1) * 128],
                      extra_waits=[(smV, v_kh)])
                evac(Sc, smS, bi, lambda e, g=g, bi=bi: e.activation(
                    P[:, KTT + g * 128:KTT + (g + 1) * 128], pb[bi][0:128, 128:256],
                    AF.Identity, bias=0.0, scale=1.0))
                tpose(bi, pb[bi][0:128, 256:384], P[:, VF + g * 128:VF + (g + 1) * 128],
                      extra_waits=[(smS, s_kv)])
                evac(Sc, smS, bi, lambda e, g=g, bi=bi: e.activation(
                    P[:, VTT + g * 128:VTT + (g + 1) * 128], pb[bi][0:128, 256:384],
                    AF.Identity, bias=0.0, scale=1.0))
            s_tm = smS.n

            if not is_prefix:
                for hh in range(H):
                    g = hh // (H // HKV)
                    slot = P[:, WSL[hh % 2]:WSL[hh % 2] + 2048]
                    dq = dma(slot, bass.AP(wq_r, hh * 128 * 2048,
                                           [[2048, 128], [1, 2048]]),
                             waits=[slotR[hh % 2]])
                    bi = hh % 2
                    for kc in range(KC):
                        mm(bi, pb[bi][0:128, 256:384], slot[:, kc * 128:(kc + 1) * 128],
                           P[:, H1F + kc * 128:H1F + (kc + 1) * 128], kc == 0, kc == KC - 1,
                           extra_waits=([(smD, dq)] if kc == 0 else []),
                           force_inc=(kc == KC - 1))
                    slotR[hh % 2] = (smT, smT.n)
                    evac(Sc, smS, bi, lambda e, hh=hh, bi=bi: e.activation(
                        P[:, QH1:QH1 + 128], pb[bi][0:128, 256:384], AF.Relu,
                        bias=sbq[:, hh:hh + 1], scale=SCALE))
                    s_q = smS.n
                    emit(V, lambda e, g=g, hh=hh: e.tensor_mul(
                        P[:, QT16 + hh * 128:QT16 + (hh + 1) * 128], P[:, QH1:QH1 + 128],
                        P[:, PP + g * 128:PP + (g + 1) * 128]),
                        [(smS, s_q), (smV, v_kt)], smV, 1)
                v_qt = smV.n
                v_st = v_qt
                for c in range(2):
                    base = (c % 2) * 64
                    for hh in range(H):
                        g = hh // (H // HKV)
                        par = st_par[g]
                        mm(4, pb[4][base:base + 64, 0:64],
                           P[:, KT_ + g * 128 + c * 64:KT_ + g * 128 + (c + 1) * 64],
                           P[:, QT16 + hh * 128 + c * 64:QT16 + hh * 128 + (c + 1) * 64],
                           True, True, extra_waits=[(smV, v_qt)])
                        evac(V, smV, 4, lambda e, base=base: e.tensor_tensor(
                            P[base:base + 64, SCC:SCC + 64], pb[4][base:base + 64, 0:64],
                            masku[:], ALU.mult), extra_waits=[(smD, d_consts)])
                        v_sc = smV.n
                        ob = 5 + (hh % 2)
                        mm(ob, pb[ob][0:128, c * 64:(c + 1) * 64],
                           P[:, STO + (g * 2 + par) * 128:STO + (g * 2 + par) * 128 + 128],
                           P[:, QT16 + hh * 128 + c * 64:QT16 + hh * 128 + (c + 1) * 64],
                           True, False, extra_waits=[(smV, v_st)])
                        mm(ob, pb[ob][0:128, c * 64:(c + 1) * 64],
                           P[base:base + 64, VTT + g * 128:VTT + (g + 1) * 128],
                           P[base:base + 64, SCC:SCC + 64], False, True,
                           extra_waits=[(smV, v_sc), (smS, s_tm)])
                        evac(Sc, smS, ob, lambda e, hh=hh, c=c, ob=ob: e.activation(
                            P[:, OF + hh * 128 + c * 64:OF + hh * 128 + (c + 1) * 64],
                            pb[ob][0:128, c * 64:(c + 1) * 64],
                            AF.Identity, bias=0.0, scale=1.0))
                    _state_update(c, s_tm)
                    v_st = smV.n
            else:
                for c in range(2):
                    _state_update(c, s_tm)

            if not is_prefix:
                s_of = smS.n
                for hh in range(H):
                    bi = 2 + (hh % 2)
                    tpose(bi, pb[bi][0:128, 384:512], P[:, OF + hh * 128:OF + (hh + 1) * 128],
                          extra_waits=[(smS, s_of)])
                    evac(Sc, smS, bi, lambda e, hh=hh, bi=bi, gc=glob_chunk: e.activation(
                        C[:, gc * 2048 + hh * 128:gc * 2048 + (hh + 1) * 128],
                        pb[bi][0:128, 384:512], AF.Identity, bias=0.0, scale=1.0))

        def _state_update(c, s_tm):
            base = (c % 2) * 64
            for g in range(HKV):
                par = st_par[g]
                mm(7, pb[7][0:128, g * 128:(g + 1) * 128],
                   P[base:base + 64, KTT + g * 128:KTT + (g + 1) * 128],
                   P[base:base + 64, VTT + g * 128:VTT + (g + 1) * 128], True, True,
                   extra_waits=[(smS, s_tm)])
                evac(V, smV, 7, lambda e, g=g, c=c, par=par: e.scalar_tensor_tensor(
                    P[:, STO + (g * 2 + 1 - par) * 128:STO + (g * 2 + 1 - par) * 128 + 128],
                    P[:, STO + (g * 2 + par) * 128:STO + (g * 2 + par) * 128 + 128],
                    P[:, PP + g * 128 + c * 64 + 63:PP + g * 128 + c * 64 + 64],
                    pb[7][0:128, g * 128:(g + 1) * 128], ALU.mult, ALU.add),
                    extra_waits=[(smG, g_init)])
            for g in range(HKV):
                st_par[g] = 1 - st_par[g]

        xp_w = []
        for pt in range(NT):
            xp_w.append((smD, dma(C[:, pt * 2048:(pt + 1) * 2048],
                                  xp[pt * 128:(pt + 1) * 128, :])))
        s_sq, v_sum = ln_batch_stats(
            [(C[:, pt * 2048:(pt + 1) * 2048], [xp_w[pt]]) for pt in range(NT)], 0)
        g_ln1p = ln_batch_finish(0, s_sq, v_sum)
        for pt in range(NT):
            attn_tile(True, None, C[:, pt * 2048:(pt + 1) * 2048], xp_w[pt], pt, g_ln1p)
        for g in range(HKV):
            par = st_par[g]
            emit(V, lambda e, g=g, par=par: e.tensor_scalar_mul(
                P[:, STO + (g * 2 + 1 - par) * 128:STO + (g * 2 + 1 - par) * 128 + 128],
                P[:, STO + (g * 2 + par) * 128:STO + (g * 2 + par) * 128 + 128],
                psc[:]), [(smV, smV.n), (smD, d_consts)] if g == 0 else [], smV, 1)
            st_par[g] = 1 - par
        vgap()

        s_sq, v_sum = ln_batch_stats(
            [(A[:, j * 2048:(j + 1) * 2048], [(smD, d_xtiles)]) for j in range(NT)], 0)
        g_ln1 = ln_batch_finish(0, s_sq, v_sum)
        for j in range(NT):
            attn_tile(False, j, A[:, j * 2048:(j + 1) * 2048], (smD, d_xtiles), j, g_ln1)
        if "d_st" in dbg_t:
            for g in range(HKV):
                dma(dbg_t["d_st"][:, g * 128:(g + 1) * 128],
                    P[:, STO + (g * 2 + st_par[g]) * 128:STO + (g * 2 + st_par[g]) * 128 + 128],
                    waits=[(smV, smV.n)])
        d_att_dump = 0
        if "d_attn" in dbg_t:
            d_att_dump = dma(dbg_t["d_attn"][:, :], C[:, :], waits=[(smS, smS.n)])

        w_att = (smS, smS.n)
        if d_att_dump:
            emit(V, lambda e: e.tensor_mul(GAP[:, 0:1], GAP[:, 2:3], GAP[:, 3:4]),
                 [(smD, d_att_dump)])
        s_sq, v_sum = ln_batch_stats(
            [(C[:, j * 2048:(j + 1) * 2048], [w_att]) for j in range(NT)], 0)
        g_ln2 = ln_batch_finish(0, s_sq, v_sum)
        v_x2l = []
        for j in range(NT):
            cj = C[:, j * 2048:(j + 1) * 2048]
            aj = A[:, j * 2048:(j + 1) * 2048]
            for q in range(4):
                ln_apply(pb[q][0:128, 0:512], cj[:, q * 512:(q + 1) * 512], 0, j, g_ln2,
                         psum_banks=[q], extra_waits=[w_att])
            s_ln2 = smS.n
            for q in range(4):
                emit(V, lambda e, q=q, cj=cj, aj=aj: e.tensor_add(
                    cj[:, q * 512:(q + 1) * 512], pb[q][0:128, 0:512],
                    aj[:, q * 512:(q + 1) * 512]),
                    [(smS, s_ln2), (smD, d_xtiles)], smV, 1)
                bank[q]["r"] = (smV, smV.n)
            v_x2l.append(smV.n)
            dma(x2sp[j * 128:(j + 1) * 128, :], cj, waits=[(smV, smV.n)])
        s_sq3, v_sum3 = ln_batch_stats(
            [(C[:, j * 2048:(j + 1) * 2048], [(smV, v_x2l[j])]) for j in range(NT)], 1)
        g_ln3 = ln_batch_finish(1, s_sq3, v_sum3)
        for j in range(NT):
            cj = C[:, j * 2048:(j + 1) * 2048]
            aj = A[:, j * 2048:(j + 1) * 2048]
            s3 = ln_apply(aj, cj, 1, j, g_ln3, extra_waits=[(smV, v_x2l[j])])
            for f in range(KC):
                bi = 4 + (f % 2)
                tpose(bi, pb[bi][0:128, 0:128], aj[:, f * 128:(f + 1) * 128],
                      extra_waits=[(smS, s3)])
                evac(Sc, smS, bi, lambda e, f=f, j=j, bi=bi: e.activation(
                    P[:, f * 1024 + j * 128:f * 1024 + (j + 1) * 128],
                    pb[bi][0:128, 0:128], AF.Identity, bias=0.0, scale=1.0))
        s_hfm = smS.n
        d_spill = smD.n
        if "d_x2" in dbg_t:
            dma(dbg_t["d_x2"][:, :], C[:, :], waits=[(smS, s_hfm)])
        if "d_hfm" in dbg_t:
            dma(dbg_t["d_hfm"][:, :], P[:, :], waits=[(smS, s_hfm)])

        for j in range(NT):
            for kc in range(KC):
                mm(6, pb[6][0:128, j * 16:j * 16 + 16],
                   P[:, kc * 1024 + j * 128:kc * 1024 + (j + 1) * 128],
                   rwTs[:, kc * 16:(kc + 1) * 16], kc == 0, kc == KC - 1,
                   extra_waits=([(smS, s_hfm), (smD, d_consts)] if kc == 0 else []))
        for j in range(NT):
            for kc in range(KC):
                mm(5, pb[5][0:128, j:j + 1],
                   P[:, kc * 1024 + j * 128:kc * 1024 + (j + 1) * 128],
                   sgw2s[:, kc:kc + 1], kc == 0, kc == KC - 1,
                   extra_waits=([(smS, s_hfm)] if kc == 0 else []))
        w6, w5 = bank[6]["w"], bank[5]["w"]
        for j in range(NT):
            emit(V, lambda e, j=j: e.reduce_max(
                ST8[:, R_NMX + j:R_NMX + j + 1], pb[6][0:128, j * 16:j * 16 + 16],
                axis=AX.X), [w6], smV, 1)
        vgap()
        v_nmx = smV.n
        emit(G, lambda e: e.tensor_scalar(
            ST8[:, R_NMN:R_NMN + 8], ST8[:, R_NMX:R_NMX + 8], -1.0, None, ALU.mult),
            [(smV, v_nmx)], smG, 1)
        g_nmn = smG.n
        for j in range(NT):
            emit(Sc, lambda e, j=j: e.activation(
                PRB[:, j * 16:(j + 1) * 16], pb[6][0:128, j * 16:j * 16 + 16],
                AF.Exp, bias=ST8[:, R_NMN + j:R_NMN + j + 1], scale=1.0),
                [(smG, g_nmn), w6] if j == 0 else [], smS, 1)
        bank[6]["r"] = (smS, smS.n)
        for j in range(NT):
            emit(Sc, lambda e, j=j: e.activation(
                CMBX[:, j * 17 + 16:j * 17 + 17], pb[5][0:128, j:j + 1],
                AF.Sigmoid, bias=0.0, scale=1.0), [w5] if j == 0 else [], smS, 1)
        bank[5]["r"] = (smS, smS.n)
        s_prb = smS.n
        if "d_prb" in dbg_t:
            dma(dbg_t["d_prb"][:, :], PRB[:, :], waits=[(smS, s_prb)])
        srcb = [PRB, WKA, WKB, WKA]
        dstb = [WKA, WKB, WKA, None]
        for it in range(4):
            for j in range(NT):
                emit(V, lambda e, it=it, j=j: e.reduce_max(
                    M4[:, it * 8 + j:it * 8 + j + 1],
                    srcb[it][:, j * 16:(j + 1) * 16], axis=AX.X),
                    [(smS, s_prb)] if (it == 0 and j == 0) else [], smV, 1)
            if it < 3:
                for j in range(NT):
                    emit(V, lambda e, it=it, j=j: e.tensor_scalar(
                        DDX[:, j * 16:(j + 1) * 16], srcb[it][:, j * 16:(j + 1) * 16],
                        M4[:, it * 8 + j:it * 8 + j + 1], None, ALU.subtract), [], smV, 1)
                for j in range(NT):
                    emit(V, lambda e, it=it, j=j: e.memset(
                        dstb[it][:, j * 16:(j + 1) * 16], -1e30), [], smV, 1)
                for j in range(NT):
                    emit(V, lambda e, it=it, j=j: e.copy_predicated(
                        dstb[it][:, j * 16:(j + 1) * 16],
                        DDX[:, j * 16:(j + 1) * 16].bitcast(U32),
                        srcb[it][:, j * 16:(j + 1) * 16]), [], smV, 1)
        v_m4 = smV.n
        emit(G, lambda e: e.tensor_add(SM4[:, 0:8], M4[:, 0:8], M4[:, 8:16]),
             [(smV, v_m4)], smG, 1)
        emit(G, lambda e: e.tensor_add(SM4[:, 8:16], M4[:, 16:24], M4[:, 24:32]), [], smG, 1)
        emit(G, lambda e: e.tensor_add(SM4[:, 16:24], SM4[:, 0:8], SM4[:, 8:16]), [], smG, 1)
        emit(G, lambda e: e.tensor_scalar_max(SM4[:, 0:8], SM4[:, 16:24], 1e-9), [], smG, 1)
        g_den = smG.n
        emit(V, lambda e: e.reciprocal(SM4[:, 8:16], SM4[:, 0:8]), [(smG, g_den)], smV, 1)
        vgap()
        for j in range(NT):
            emit(V, lambda e, j=j: e.tensor_scalar(
                D4[:, j * 16:(j + 1) * 16], PRB[:, j * 16:(j + 1) * 16],
                M4[:, 24 + j:24 + j + 1], None, ALU.subtract), [], smV, 1)
        v_d4 = smV.n
        for j in range(NT):
            emit(Sc, lambda e, j=j: e.activation(
                SGN[:, j * 16:(j + 1) * 16], D4[:, j * 16:(j + 1) * 16],
                AF.Sign, bias=0.0, scale=-1.0),
                [(smV, v_d4)] if j == 0 else [], smS, 1)
        s_sgn = smS.n
        for j in range(NT):
            emit(V, lambda e, j=j: e.tensor_scalar(
                K1[:, j * 16:(j + 1) * 16], SGN[:, j * 16:(j + 1) * 16], -1.0, None,
                ALU.mult), [(smS, s_sgn)] if j == 0 else [], smV, 1)
        for j in range(NT):
            emit(V, lambda e, j=j: e.tensor_scalar(
                D4[:, j * 16:(j + 1) * 16], K1[:, j * 16:(j + 1) * 16], 1.0, None,
                ALU.add), [], smV, 1)
        for j in range(NT):
            emit(V, lambda e, j=j: e.tensor_scalar_min(
                SGN[:, j * 16:(j + 1) * 16], D4[:, j * 16:(j + 1) * 16], 1.0), [], smV, 1)
        for j in range(NT):
            emit(V, lambda e, j=j: e.tensor_mul(
                K1[:, j * 16:(j + 1) * 16], PRB[:, j * 16:(j + 1) * 16],
                SGN[:, j * 16:(j + 1) * 16]), [], smV, 1)
        for j in range(NT):
            emit(V, lambda e, j=j: e.tensor_scalar_mul(
                CMBX[:, j * 17:j * 17 + 16], K1[:, j * 16:(j + 1) * 16],
                SM4[:, 8 + j:8 + j + 1]), [], smV, 1)
        v_cmb = smV.n
        for j in range(NT):
            bi = 6 + (j % 2)
            tpose(bi, pb[bi][0:17, 0:128], CMBX[:, j * 17:(j + 1) * 17],
                  extra_waits=[(smV, v_cmb), (smS, s_prb)])
            evac(Sc, smS, bi, lambda e, j=j, bi=bi: e.activation(
                CMBF[:, j * 128:(j + 1) * 128], pb[bi][0:17, 0:128],
                AF.Identity, bias=0.0, scale=1.0))
        s_cmbf = smS.n
        if "d_cmbf" in dbg_t:
            dma(dbg_t["d_cmbf"][:, :], CMBF[:, :], waits=[(smS, s_cmbf)])

        for j in range(NT):
            emit(V, lambda e, j=j: e.memset(A[:, j * 2048:(j + 1) * 2048], 0.0),
                 [(smT, smT.n)] if j == 0 else [], smV, 1)
        v_a0 = smV.n

        last_ztc_read = 0
        sqd_read = 0
        DW = {"d": 0}
        for pe in range(NPE):
            t_pe0 = smT.n
            row = min(pe, 16)
            d_sel = dma(selE[:], bass.AP(sel_i, row * 128, [[17 * 128, 17], [1, 128]]),
                        waits=[(smT, smT.n)])
            for tt in range(2):
                mm(6, pb[6][0:128, 0:512], selE[:, :], CMBF[:, tt * 512:(tt + 1) * 512],
                   True, True, extra_waits=[(smD, d_sel), (smS, s_cmbf)])
                evac(Sc, smS, 6, lambda e, tt=tt: e.activation(
                    C[:, CBC0 + tt * 512:CBC0 + (tt + 1) * 512], pb[6][0:128, 0:512],
                    AF.Identity, bias=0.0, scale=1.0),
                    extra_waits=[(smD, d_spill), (smV, last_ztc_read)])
            s_cbc = smS.n
            if "d_cbc" in dbg_t and pe == 0:
                dma(dbg_t["d_cbc"][:, :], C[:, CBC0:CBC0 + 1024], waits=[(smS, s_cbc)])
            for i in range(NIC):
                slot = C[:, WSF[0]:WSF[0] + 2048]
                slot2 = C[:, WSF[1]:WSF[1] + 2048]
                si = pe * NIC + i
                gsrc = bass.AP(wg_r, si * 128 * 2048, [[2048, 128], [1, 2048]])
                usrc = bass.AP(wu_r, si * 128 * 2048, [[2048, 128], [1, 2048]])
                dg = dma(slot, gsrc, waits=[slotR[2], (smD, d_spill)])
                for tt in range(2):
                    for kc in range(KC):
                        mm(tt, pb[tt][0:128, 0:512], slot[:, kc * 128:(kc + 1) * 128],
                           P[:, kc * 1024 + tt * 512:kc * 1024 + (tt + 1) * 512],
                           kc == 0, kc == KC - 1,
                           extra_waits=([(smD, dg), (smS, s_hfm)] if kc == 0 else []),
                           force_inc=(kc == KC - 1 and tt == 1))
                slotR[2] = (smT, smT.n)
                du = dma(slot2, usrc, waits=[slotR[3]])
                for tt in range(2):
                    for kc in range(KC):
                        mm(2 + tt, pb[2 + tt][0:128, 0:512], slot2[:, kc * 128:(kc + 1) * 128],
                           P[:, kc * 1024 + tt * 512:kc * 1024 + (tt + 1) * 512],
                           kc == 0, kc == KC - 1,
                           extra_waits=([(smD, du)] if kc == 0 else []),
                           force_inc=(kc == KC - 1 and tt == 1))
                slotR[3] = (smT, smT.n)
                for tt in range(2):
                    evac(Sc, smS, tt, lambda e, tt=tt: e.activation(
                        pb[4][0:128, 0:512], pb[tt][0:128, 0:512], AF.Silu,
                        bias=0.0, scale=1.0), extra_waits=[bank[4]["r"]])
                    bank[4]["w"] = (smS, smS.n)
                    evac(Sc, smS, 2 + tt, lambda e, tt=tt: e.activation(
                        SQD[:], pb[2 + tt][0:128, 0:512], AF.Identity,
                        bias=0.0, scale=1.0), extra_waits=[(smV, sqd_read)])
                    s_zin = smS.n
                    emit(V, lambda e: e.tensor_tensor(
                        pb[5][0:128, 0:512], pb[4][0:128, 0:512], SQD[:], ALU.mult),
                        [(smS, s_zin), bank[5]["r"]], smV, 1)
                    bank[4]["r"] = (smV, smV.n)
                    bank[5]["w"] = (smV, smV.n)
                    sqd_read = smV.n
                    emit(V, lambda e, i=i, tt=tt: e.tensor_tensor(
                        C[:, ZTC0 + i * 1024 + tt * 512:ZTC0 + i * 1024 + (tt + 1) * 512],
                        pb[5][0:128, 0:512],
                        C[:, CBC0 + tt * 512:CBC0 + (tt + 1) * 512], ALU.mult),
                        [(smS, s_cbc), (smT, t_pe0)], smV, 1)
                    bank[5]["r"] = (smV, smV.n)
            v_ztc = smV.n
            last_ztc_read = smV.n
            if "d_ztc" in dbg_t and pe == 0:
                dma(dbg_t["d_ztc"][:, :], C[:, ZTC0:ZTC0 + 11264], waits=[(smV, v_ztc)])
            for q2 in range(2):
                for fh in range(2):
                    for i in range(NIC):
                        slot = C[:, WSF[i % 2]:WSF[i % 2] + 2048]
                        if pe < E:
                            dsrc = bass.AP(wd, pe * IMOE * DM + i * 128 * DM + fh * 1024,
                                           [[DM, 128], [1, 1024]])
                        else:
                            i0 = (pe - E) * IMOE + i * 128
                            dsrc = bass.AP(shd, i0 * DM + fh * 1024,
                                           [[DM, 128], [1, 1024]])
                        dw = dma(slot[:, 0:1024], dsrc, waits=[slotR[2 + i % 2]])
                        for f in range(8):
                            mm(f, pb[f][0:128, 0:512], slot[:, f * 128:(f + 1) * 128],
                               C[:, ZTC0 + i * 1024 + q2 * 512:ZTC0 + i * 1024 + (q2 + 1) * 512],
                               i == 0, i == NIC - 1,
                               extra_waits=[(smD, dw), (smV, v_ztc)],
                               force_inc=(f == 7))
                        slotR[2 + i % 2] = (smT, smT.n)
                    for f in range(8):
                        fg = fh * 8 + f
                        evac(V, smV, f, lambda e, fg=fg, q2=q2, f=f: e.tensor_add(
                            A[:, fg * 1024 + q2 * 512:fg * 1024 + (q2 + 1) * 512],
                            A[:, fg * 1024 + q2 * 512:fg * 1024 + (q2 + 1) * 512],
                            pb[f][0:128, 0:512]),
                            extra_waits=[(smV, v_a0), (smD, DW["d"])])
            if "d_y0" in dbg_t and pe == 0:
                DW["d"] = dma(dbg_t["d_y0"][:, :], A[:, :], waits=[(smV, smV.n)])
        v_ffn = smV.n
        if "d_ffn" in dbg_t:
            dma(dbg_t["d_ffn"][:, :], A[:, :], waits=[(smV, v_ffn)])

        for j in range(NT):
            for f in range(KC):
                bi = 4 + (f % 2)
                tpose(bi, pb[bi][0:128, 0:128],
                      A[:, f * 1024 + j * 128:f * 1024 + (j + 1) * 128],
                      extra_waits=[(smV, v_ffn)])
                evac(Sc, smS, bi, lambda e, f=f, j=j, bi=bi: e.activation(
                    C[:, j * 2048 + f * 128:j * 2048 + (f + 1) * 128],
                    pb[bi][0:128, 0:128], AF.Identity, bias=0.0, scale=1.0))
        s_ftm = smS.n
        s_sq4, v_sum4 = ln_batch_stats(
            [(C[:, j * 2048:(j + 1) * 2048], [(smS, s_ftm)]) for j in range(NT)], 2)
        g_ln4 = ln_batch_finish(2, s_sq4, v_sum4)
        for j in range(NT):
            cj = C[:, j * 2048:(j + 1) * 2048]
            for q in range(4):
                ln_apply(pb[q][0:128, 0:512], cj[:, q * 512:(q + 1) * 512], 2, j, g_ln4,
                         psum_banks=[q], extra_waits=[(smS, s_ftm)])
            s_ln4 = smS.n
            dx2 = dma(P[:, 2048:4096], x2sp[j * 128:(j + 1) * 128, :],
                      waits=[(smV, smV.n), (smT, smT.n)])
            for q in range(4):
                emit(V, lambda e, q=q: e.tensor_add(
                    P[:, q * 512:(q + 1) * 512], pb[q][0:128, 0:512],
                    P[:, 2048 + q * 512:2048 + (q + 1) * 512]),
                    [(smS, s_ln4), (smD, dx2)], smV, 1)
                bank[q]["r"] = (smV, smV.n)
            v_out = smV.n
            dma(out[j * 128:(j + 1) * 128, :], P[:, 0:2048], waits=[(smV, v_out)])
        d_final = smD.n
        emit(G, lambda e: e.memset(GAP[:, 0:1], 0.0), [(smD, d_final)], smG, 1)

        with nc.Block() as block:
            def runner(plan):
                def fn(eng):
                    last = {}
                    for (w, f, inc, amt) in plan.steps:
                        for (s_, v_) in w:
                            if last.get(s_.name, -1) < v_:
                                eng.wait_ge(s_.h, v_)
                                last[s_.name] = v_
                        ins = f(eng)
                        if inc is not None:
                            ins.then_inc(inc.h, amt)
                return fn
            block.sync(runner(Dq))
            block.tensor(runner(T))
            block.vector(runner(V))
            block.scalar(runner(Sc))
            block.gpsimd(runner(G))
    return nc


def _host_inputs(inputs):
    x = np.ascontiguousarray(np.asarray(inputs["L0_attn_hidden_states"], dtype=np.float32))
    f32 = lambda a: np.ascontiguousarray(np.asarray(a, dtype=np.float32))
    shared = {
        "wq_r": f32(np.asarray(inputs["L0_attn_Wq"]).reshape(16, 128, 16, 128)
                     .transpose(0, 3, 2, 1).reshape(16, 128, 2048)),
        "wkv_r": f32(np.concatenate([
            np.asarray(inputs["L0_attn_Wk"]).reshape(4, 128, 16, 128)
              .transpose(0, 3, 2, 1).reshape(4, 128, 2048),
            np.asarray(inputs["L0_attn_Wv"]).reshape(4, 128, 16, 128)
              .transpose(0, 3, 2, 1).reshape(4, 128, 2048)], 0)),
        "bq2": f32(np.asarray(inputs["L0_attn_bq"]).reshape(16, 128).T),
        "bk2": f32(np.asarray(inputs["L0_attn_bk"]).reshape(4, 128).T),
        "bv2": f32(np.asarray(inputs["L0_attn_bv"]).reshape(4, 128).T),
        "rwT": f32(np.asarray(inputs["L0_ffn_router_w"]).T),
        "wg_r": f32(np.concatenate([
            np.asarray(inputs["L0_ffn_w_gate"]).reshape(16, 16, 128, 11, 128)
              .transpose(0, 3, 2, 1, 4).reshape(176, 128, 2048),
            np.asarray(inputs["L0_ffn_sh_gate"]).reshape(16, 128, 44, 128)
              .transpose(2, 1, 0, 3).reshape(44, 128, 2048)], 0)),
        "wu_r": f32(np.concatenate([
            np.asarray(inputs["L0_ffn_w_up"]).reshape(16, 16, 128, 11, 128)
              .transpose(0, 3, 2, 1, 4).reshape(176, 128, 2048),
            np.asarray(inputs["L0_ffn_sh_up"]).reshape(16, 128, 44, 128)
              .transpose(2, 1, 0, 3).reshape(44, 128, 2048)], 0)),
        "wd": f32(inputs["L0_ffn_w_down"]),
        "shd": f32(inputs["L0_ffn_sh_down"]),
        "sgw2": f32(np.asarray(inputs["L0_ffn_sh_gate_w"]).reshape(16, 128).T),
        "ident": np.eye(128, dtype=np.float32),
        "masku": np.triu(np.ones((64, 64), dtype=np.float32)),
        "sel_all": np.kron(np.eye(17, dtype=np.float32), np.ones((1, 128), np.float32)),
    }
    in_maps = []
    for core in range(8):
        b, h = core // 2, core % 2
        m = dict(shared)
        m["xs"] = np.ascontiguousarray(x[b, 1024 * h:1024 * (h + 1), :])
        m["xp"] = np.ascontiguousarray(x[b, 0:1024, :])
        m["pscale"] = np.full((128, 1), float(h), dtype=np.float32)
        in_maps.append(m)
    return in_maps


def kernel(**inputs):
    in_maps = _host_inputs(inputs)
    nc = build_nc()
    res = run_bass_kernel_spmd(nc, in_maps, core_ids=list(range(8)))
    out = np.empty((B, S, DM), dtype=np.float32)
    for core in range(8):
        b, h = core // 2, core % 2
        out[b, 1024 * h:1024 * (h + 1), :] = res.results[core]["out"]
    return out
